# version 08 (final)
# baseline (speedup 1.0000x reference)
import numpy as np
from contextlib import ExitStack
import concourse.bass as bass
import concourse.mybir as mybir
from concourse.bass_utils import run_bass_kernel_spmd

F32 = mybir.dt.float32
BF16 = mybir.dt.bfloat16
AF = mybir.ActivationFunctionType
ALU = mybir.AluOpType
AXX = mybir.AxisListType.X

MAGIC = 12582912.0
D = 2048
T = 2048
H = 32
PAIRS = 16
FH = 5632
FHP = FH // 128
DP = D // 128
JT = 16
NCH = 4
EPS_LN = 1e-5
EPS_ATT = 1e-12
EPS_Q = 1e-8
NW_EL = float(FH * D)

ENGS = ("sy", "gp", "sc", "ve", "te")


class Sem:
    __slots__ = ("h", "n", "name")

    def __init__(self, h, name):
        self.h = h
        self.n = 0
        self.name = name


class Prog:
    def __init__(self, nc, es):
        self.nc = nc
        self.ops = {e: [] for e in ENGS}
        self.last_wait = {e: {} for e in ENGS}
        self.es = es
        self.sems = []
        self.seq = 0
        self.marks = {}

    def mark(self, name):
        self.marks[name] = self.seq

    def sem(self, name):
        s = Sem(self.es.enter_context(self.nc.semaphore(name)), name)
        self.sems.append(s)
        return s

    def wait(self, eng, *sems):
        for s in sems:
            self.wait_tag(eng, (s, s.n))

    def wait_tag(self, eng, tag):
        if tag is None:
            return
        s, tgt = tag
        if tgt <= 0:
            return
        if self.last_wait[eng].get(s, -1) >= tgt:
            return
        self.last_wait[eng][s] = tgt
        self.seq += 1
        self.ops[eng].append(("w", s.h, tgt, self.seq))

    def do(self, eng, fn, inc=None, k=1):
        self.seq += 1
        if inc is not None:
            inc.n += k
            self.ops[eng].append(("i", fn, inc.h, k, self.seq))
            return (inc, inc.n)
        self.ops[eng].append(("f", fn, self.seq))
        return None

    def dma(self, eng, q, out_ap, in_ap, **kw):
        return self.do(eng, lambda e: e.dma_start(out=out_ap, in_=in_ap, **kw),
                       inc=q, k=16)

    def barrier(self, engs=ENGS):
        snap = [(s, s.n) for s in self.sems if s.n > 0]
        for e in engs:
            for s, n in snap:
                self.wait_tag(e, (s, n))

    def emit(self, stop_seq=None):
        selfsem = {e: self.es.enter_context(self.nc.semaphore(f"ss_{e}"))
                   for e in ("ve", "sc")}
        with self.nc.Block() as block:
            def mk(eng_name):
                def body(eng):
                    k = 0
                    ss = selfsem.get(eng_name)
                    qsems = {s.h for s in self.sems if s.name.startswith("q")}
                    cnt = {}
                    prev = None
                    for op in self.ops[eng_name]:
                        if stop_seq is not None and op[-1] > stop_seq:
                            break
                        if op[0] == "w":
                            eng.wait_ge(op[1], op[2])
                            continue
                        is_dma = op[0] == "i" and op[2] in qsems
                        if ss is not None and prev is not None and not is_dma:
                            eng.wait_ge(prev[0], prev[1])
                        if op[0] == "f":
                            ins = op[1](eng)
                            if ss is not None and not is_dma:
                                ins.then_inc(ss, 1)
                                k += 1
                                prev = (ss, k)
                        else:
                            ins = op[1](eng).then_inc(op[2], op[3])
                            if ss is not None and not is_dma:
                                cnt[op[2]] = cnt.get(op[2], 0) + op[3]
                                prev = (op[2], cnt[op[2]])
                return body
            block.sync(mk("sy"))
            block.gpsimd(mk("gp"))
            block.scalar(mk("sc"))
            block.vector(mk("ve"))
            block.tensor(mk("te"))


def build(stop=None):
    nc = bass.Bass("TRN2", target_bir_lowering=False)
    es = ExitStack()

    ap = lambda t: t.ap()
    xT = ap(nc.dram_tensor("xT", [D, T], F32, kind="ExternalInput"))
    wqT = ap(nc.dram_tensor("wqT", [D, D], F32, kind="ExternalInput"))
    wkT = ap(nc.dram_tensor("wkT", [D, D], F32, kind="ExternalInput"))
    wvT = ap(nc.dram_tensor("wvT", [D, D], F32, kind="ExternalInput"))
    woT = ap(nc.dram_tensor("woT", [D, D], F32, kind="ExternalInput"))
    wfq = ap(nc.dram_tensor("wfq", [128, PAIRS * 128], F32, kind="ExternalInput"))
    wfk = ap(nc.dram_tensor("wfk", [128, PAIRS * 128], F32, kind="ExternalInput"))
    w1T = ap(nc.dram_tensor("w1T", [D, FH], F32, kind="ExternalInput"))
    w2T = ap(nc.dram_tensor("w2T", [D, FH], F32, kind="ExternalInput"))
    w3T = ap(nc.dram_tensor("w3T", [FH, D], F32, kind="ExternalInput"))
    mask2 = ap(nc.dram_tensor("mask2", [128, 256], F32, kind="ExternalInput"))
    ident_in = ap(nc.dram_tensor("ident", [128, 128], F32, kind="ExternalInput"))
    cmask = ap(nc.dram_tensor("cmask", [128, 4], F32, kind="ExternalInput"))
    outT = ap(nc.dram_tensor("outT", [D, T], F32, kind="ExternalOutput"))

    wq_bf = ap(nc.dram_tensor("wq_bf", [D, D], BF16))
    wk_bf = ap(nc.dram_tensor("wk_bf", [D, D], BF16))
    wv_bf = ap(nc.dram_tensor("wv_bf", [D, D], BF16))
    wo_bf = ap(nc.dram_tensor("wo_bf", [D, D], BF16))
    eq_s = ap(nc.dram_tensor("eq_s", [JT, PAIRS, 128, 256], BF16))
    fkT_s = ap(nc.dram_tensor("fkT_s", [JT, PAIRS, 128, 128], BF16))
    fk_s = ap(nc.dram_tensor("fk_s", [JT, 128, D], BF16))
    v_s = ap(nc.dram_tensor("v_s", [JT, 128, H * 65], BF16))
    resid1_s = ap(nc.dram_tensor("resid1_s", [D, T], F32))
    hid_s = ap(nc.dram_tensor("hid_s", [NCH, FH, 512], BF16))
    qw3_s = ap(nc.dram_tensor("qw3_s", [4, FH, 512], BF16))
    cc_in_t = nc.dram_tensor("cc_in", [128, PAIRS * 65], F32)
    cc_out_t = nc.dram_tensor("cc_out", [512, PAIRS * 65], F32)

    P = Prog(nc, es)
    qS = P.sem("qS")
    qG = P.sem("qG")
    qA = P.sem("qA")
    eT = P.sem("eT")
    eV = P.sem("eV")
    eA = P.sem("eA")
    eG = P.sem("eG")
    ccs = P.sem("ccs")
    qOUT = P.sem("qOUT")

    sbt = lambda name, shape, dt: es.enter_context(nc.sbuf_tensor(name, shape, dt))
    pst = lambda name, shape, dt: es.enter_context(nc.psum_tensor(name, shape, dt))

    PB = [pst(f"PB{i}", [128, 512], F32) for i in range(6)]
    PT = [pst(f"PT{i}", [128, 1024], BF16) for i in range(2)]

    cst = sbt("cst", [128, 8], F32)
    ones_bf = sbt("ones_bf", [128, 1], BF16)
    zcol_bf = sbt("zcol_bf", [1, 1], BF16)
    zrow_bf = sbt("zrow_bf", [1, 512], BF16)
    ones_f = sbt("ones_f", [1, 128], F32)
    onecol_f = sbt("onecol_f", [128, 1], F32)
    crow = sbt("crow", [1, 8], F32)
    smask = sbt("smask", [128, 256], F32)
    sident = sbt("sident", [128, 128], BF16)
    scmask = sbt("scmask", [128, 4], F32)
    s_init = sbt("s_init", [128, PAIRS * 65], F32)
    s_snap = sbt("s_snap", [128, PAIRS * 65], BF16)
    r_bc = sbt("r_bc", [128, 4], F32)
    grow = sbt("grow", [1, 8], F32)
    statrow = sbt("statrow", [1, 1536], F32)
    ab_sb = sbt("ab_sb", [128, 1024], F32)
    gpart = sbt("gpart", [128, 528], F32)
    gcol = sbt("gcol", [128, 4], F32)

    P.dma("gp", qG, sident[:, :], ident_in)
    P.dma("sy", qS, smask[:, :], mask2)
    P.dma("sy", qS, scmask[:, :], cmask)

    for a_, v_ in ((cst[:, 0:1], 1.0), (cst[:, 1:2], MAGIC),
                   (ones_bf[:, :], 1.0), (ones_f[:, :], 1.0),
                   (onecol_f[:, :], 1.0), (zcol_bf[:, :], 0.0),
                   (zrow_bf[:, :], 0.0),
                   (crow[:, 0:1], EPS_LN), (crow[:, 1:2], -0.5)):
        P.do("ve", lambda v, a=a_, x=v_: v.memset(a, x))
    P.do("ve", lambda v: v.memset(gpart[:, :], 0.0), inc=eV)
    consts_tag = (eV, eV.n)

    sfq = sbt("sfq", [128, PAIRS * 128], BF16)
    sfk = sbt("sfk", [128, PAIRS * 128], BF16)
    P.dma("gp", qG, sfq[:, :], wfq)
    P.dma("gp", qG, sfk[:, :], wfk)
    for src, dst in ((wkT, wk_bf), (wqT, wq_bf), (wvT, wv_bf), (woT, wo_bf)):
        for i in range(4):
            P.dma("gp", qG, dst[512 * i:512 * (i + 1), :],
                  src[512 * i:512 * (i + 1), :])
    wk_ready = (qG, qG.n - 48)
    wq_ready = (qG, qG.n - 32)
    wv_ready = (qG, qG.n - 16)
    wo_ready = (qG, qG.n)

    esG = ExitStack()
    gbufA = [esG.enter_context(nc.sbuf_tensor(f"gbufA{i}", [128, 512], F32,
                                               side="right")) for i in range(2)]
    gbufV = [esG.enter_context(nc.sbuf_tensor(f"gbufV{i}", [128, 512], F32,
                                              side="right")) for i in range(2)]
    gjA = esG.enter_context(nc.sbuf_tensor("gjA", [128, 512], BF16, side="right"))
    gjV = esG.enter_context(nc.sbuf_tensor("gjV", [128, 512], BF16, side="right"))
    gwA, gwV = [], []
    for m, w in enumerate((w1T, w2T, w3T)):
        flat = w.rearrange("a b -> (a b)").rearrange("(n p c) -> n p c",
                                                     p=128, c=512)
        for i in range(flat.shape[0]):
            lane = (gwA if (m == 0 or (m == 2 and i % 2 == 0)) else gwV)
            lane.append((m, i, flat[i]))
    gstate = {"ia": 0, "iv": 0, "relA": [None, None], "relV": [None, None]}

    def gamma_step():
        ia, iv = gstate["ia"], gstate["iv"]
        if ia <= iv and ia < len(gwA):
            gstate["ia"] = ia + 1
            m, i, s_ = gwA[ia]
            buf = gbufA[ia % 2]
            P.wait_tag("sc", gstate["relA"][ia % 2])
            P.dma("sc", qA, buf[:, :], s_)
            P.wait_tag("sc", (qA, qA.n))
            gstate["relA"][ia % 2] = P.do(
                "sc", lambda s, b=buf, mm=m, ii=i: s.activation(
                    gjA[:, :], b[:, :], AF.Abs,
                    accum_out=gpart[:, mm * 176 + ii:mm * 176 + ii + 1]), inc=eA)
        elif iv < len(gwV):
            gstate["iv"] = iv + 1
            m, i, s_ = gwV[iv]
            buf = gbufV[iv % 2]
            P.wait_tag("gp", gstate["relV"][iv % 2])
            t = P.dma("gp", qG, buf[:, :], s_)
            P.wait_tag("ve", t)
            gstate["relV"][iv % 2] = P.do(
                "ve", lambda v, b=buf, mm=m, ii=i: v.scalar_tensor_tensor(
                    gjV[:, :], b[:, :], -1.0, b[:, :], ALU.mult, ALU.max,
                    accum_out=gpart[:, mm * 176 + ii:mm * 176 + ii + 1]), inc=eV)

    def gamma_pending():
        return gstate["ia"] < len(gwA) or gstate["iv"] < len(gwV)

    def ln_rows_and_bcast(w, sum_row, sq_row, bc_a, bc_b, tag_stats):
        P.wait_tag("ve", tag_stats)
        P.do("ve", lambda v: v.tensor_scalar_mul(
            statrow[0:1, 0:w], sum_row, 1.0 / D))
        P.do("ve", lambda v: v.tensor_mul(
            statrow[0:1, 512:512 + w], statrow[0:1, 0:w], statrow[0:1, 0:w]))
        rv = P.do("ve", lambda v: v.scalar_tensor_tensor(
            statrow[0:1, 512:512 + w], sq_row, 1.0 / D,
            statrow[0:1, 512:512 + w], ALU.mult, ALU.subtract), inc=eV)
        P.wait_tag("sc", rv)
        P.do("sc", lambda s: s.activation(
            statrow[0:1, 1024:1024 + w], statrow[0:1, 512:512 + w], AF.Ln,
            bias=crow[0:1, 0:1]))
        ra = P.do("sc", lambda s: s.activation(
            statrow[0:1, 512:512 + w], statrow[0:1, 1024:1024 + w], AF.Exp,
            scale=crow[0:1, 1:2]), inc=eA)
        P.wait_tag("ve", ra)
        rb = P.do("ve", lambda v: v.scalar_tensor_tensor(
            statrow[0:1, 1024:1024 + w], statrow[0:1, 0:w], -1.0,
            statrow[0:1, 512:512 + w], ALU.mult, ALU.mult), inc=eV)
        P.wait_tag("te", rb)
        tb = P.do("te", lambda t: (
            t.matmul(bc_a, ones_f[0:1, :], statrow[0:1, 512:512 + w],
                     start=True, stop=True, skip_group_check=True),
            t.matmul(bc_b, ones_f[0:1, :], statrow[0:1, 1024:1024 + w],
                     start=True, stop=True, skip_group_check=True))[-1], inc=eT)
        P.wait_tag("sc", tb)
        P.wait_tag("sc", (eV, eV.n))
        ev = P.do("sc", lambda s: (
            s.copy(ab_sb[:, 0:w], bc_a),
            s.copy(ab_sb[:, 512:512 + w], bc_b))[-1], inc=eA)
        return ev

    esBig = ExitStack()
    hT = esBig.enter_context(nc.sbuf_tensor("hT", [128, DP * T], BF16))
    esA1 = ExitStack()
    xbuf = esA1.enter_context(nc.sbuf_tensor("xbuf", [128, DP * 512], F32))
    sxb = [esA1.enter_context(nc.sbuf_tensor(f"sxb{i}", [128, 256], BF16))
           for i in range(2)]
    sx2 = [esA1.enter_context(nc.sbuf_tensor(f"sx2{i}", [128, 256], BF16))
           for i in range(2)]
    tmpa = [esA1.enter_context(nc.sbuf_tensor(f"tmpa{i}", [128, 256], F32))
            for i in range(2)]

    xload_rel = None
    for ch in range(NCH):
        P.wait_tag("sy", xload_rel)
        for dp in range(DP):
            P.dma("sy", qS, xbuf[:, dp * 512:(dp + 1) * 512],
                  xT[dp * 128:(dp + 1) * 128, ch * 512:(ch + 1) * 512])
        xchunk_ld = (qS, qS.n)
        P.wait_tag("ve", xchunk_ld)
        P.wait_tag("sc", xchunk_ld)
        apply_tag = None
        for half in range(2):
            co = half * 256
            relv = [None, None]
            rela = [None, None]
            P.wait_tag("te", consts_tag)
            P.wait_tag("te", (eA, eA.n))
            P.do("te", lambda t: t.matmul(
                PB[0][0:1, 0:256], zcol_bf[0:1, 0:1], zrow_bf[0:1, 0:256],
                start=True, stop=False, skip_group_check=True))
            P.do("te", lambda t: t.matmul(
                PB[1][0:1, 0:256], zcol_bf[0:1, 0:1], zrow_bf[0:1, 0:256],
                start=True, stop=False, skip_group_check=True))
            for dp in range(DP):
                b = dp % 2
                src = xbuf[:, dp * 512 + co: dp * 512 + co + 256]
                P.wait_tag("ve", relv[b])
                cv = P.do("ve", lambda v, s=src, bb=b: v.tensor_copy(
                    sxb[bb][:, :], s), inc=eV)
                P.wait_tag("sc", rela[b])
                ca = P.do("sc", lambda s, a=src, bb=b: s.activation(
                    sx2[bb][:, :], a, AF.Square), inc=eA)
                P.wait_tag("te", cv)
                P.wait_tag("te", ca)
                relv[b] = P.do("te", lambda t, bb=b, d=dp: t.matmul(
                    PB[0][0:1, 0:256], ones_bf[:, :], sxb[bb][:, :],
                    start=False, stop=(d == DP - 1),
                    skip_group_check=True), inc=eT)
                rela[b] = P.do("te", lambda t, bb=b, d=dp: t.matmul(
                    PB[1][0:1, 0:256], ones_bf[:, :], sx2[bb][:, :],
                    start=False, stop=(d == DP - 1),
                    skip_group_check=True), inc=eT)
            ev = ln_rows_and_bcast(256, PB[0][0:1, 0:256], PB[1][0:1, 0:256],
                                   PB[1][:, 256:512], PB[0][:, 256:512],
                                   rela[(DP - 1) % 2])
            P.wait_tag("ve", ev)
            for dp in range(DP):
                b = dp % 2
                src = xbuf[:, dp * 512 + co: dp * 512 + co + 256]
                P.do("ve", lambda v, s=src, bb=b: v.tensor_mul(
                    tmpa[bb][:, :], s, ab_sb[:, 0:256]))
                apply_tag = P.do("ve", lambda v, bb=b, d=dp, c=ch, h=half:
                                 v.tensor_add(
                                     hT[:, d * T + c * 512 + h * 256:
                                        d * T + c * 512 + h * 256 + 256],
                                     tmpa[bb][:, :], ab_sb[:, 512:768]),
                                 inc=eV)
                if dp % 2 == 1:
                    gamma_step()
        xload_rel = apply_tag

    P.mark("A1")
    P.barrier()
    esA1.close()

    fkF = esBig.enter_context(nc.sbuf_tensor("fkF", [128, JT * D], BF16))
    esA2 = ExitStack()
    wcol = [esA2.enter_context(nc.sbuf_tensor(f"wcol{i}", [128, DP * 128], BF16))
            for i in range(2)]
    kqT = esA2.enter_context(nc.sbuf_tensor("kqT", [128, T], BF16))
    ekf = esA2.enter_context(nc.sbuf_tensor("ekf", [128, 512], F32))
    ektr = esA2.enter_context(nc.sbuf_tensor("ektr", [128, 256], F32))
    den8 = esA2.enter_context(nc.sbuf_tensor("den8", [128, 8], F32))
    fkTst = [esA2.enter_context(nc.sbuf_tensor(f"fkTst{i}", [128, 128], BF16))
             for i in range(2)]
    eqst = [esA2.enter_context(nc.sbuf_tensor(f"eqst{i}", [128, 1024], BF16))
            for i in range(2)]
    for i in range(2):
        P.do("ve", lambda v, b2=eqst[i]: v.memset(b2[:, :], 0.0),
             inc=(eV if i == 1 else None))
    eqz_tag = (eV, eV.n)
    P.wait_tag("sc", eqz_tag)

    def load_wcol(pair, src_bf, buf, rel_tag):
        P.wait_tag("sy", rel_tag)
        for g in range(4):
            P.dma("sy", qS,
                  buf[:, g * 4 * 128:(g + 1) * 4 * 128]
                  .rearrange("p (a i) -> p a i", a=4),
                  src_bf[g * 512:(g + 1) * 512, pair * 128:(pair + 1) * 128]
                  .rearrange("(a p) i -> p a i", p=128))
        return (qS, qS.n)

    def proj_pair(buf, out_sb, ld_tag, prev_ev):
        P.wait_tag("te", ld_tag)
        P.wait_tag("te", prev_ev)
        evtag = None
        ev_hist = [None, None]
        for n in range(4):
            bank = PB[2 + (n % 2)]
            P.wait_tag("te", ev_hist[n % 2])
            mmtag = None
            for kp in range(DP):
                mmtag = P.do("te", lambda t, k=kp, nn=n, bk=bank: t.matmul(
                    bk[:, :], buf[:, k * 128:(k + 1) * 128],
                    hT[:, k * T + nn * 512:k * T + (nn + 1) * 512],
                    start=(k == 0), stop=(k == DP - 1),
                    skip_group_check=True), inc=eT)
            P.wait_tag("sc", mmtag)
            evtag = P.do("sc", lambda s, nn=n, bk=bank: s.copy(
                out_sb[:, nn * 512:(nn + 1) * 512], bk[:, :]), inc=eA)
            ev_hist[n % 2] = evtag
        return evtag

    wrel = [None, None]
    ev_prev = None
    fkT_wr = []
    fk_rel = None
    trel = [None, None]
    for pair in range(PAIRS):
        b = pair % 2
        if pair == 0:
            P.wait_tag("sy", wk_ready)
        ld = load_wcol(pair, wk_bf, wcol[b], wrel[b])
        ev = proj_pair(wcol[b], kqT, ld, ev_prev)
        ev_prev = ev
        wrel[b] = ev
        P.wait_tag("te", ev)
        for g in range(4):
            P.wait_tag("te", fk_rel)
            mmtag = None
            for j4 in range(4):
                jt = g * 4 + j4
                mmtag = P.do("te", lambda t, j=j4, jj=jt, p=pair: t.matmul(
                    PB[4][:, j * 128:(j + 1) * 128],
                    kqT[:, jj * 128:(jj + 1) * 128],
                    sfk[:, p * 128:(p + 1) * 128],
                    start=True, stop=True, skip_group_check=True), inc=eT)
            P.wait_tag("sc", mmtag)
            P.wait_tag("sc", fk_rel)
            ex = P.do("sc", lambda s: s.activation(ekf[:, :], PB[4][:, :],
                                                   AF.Exp), inc=eA)
            P.wait_tag("ve", ex)
            srcv = ekf[:, :].rearrange("p (a b f) -> p a b f", a=4, b=2)
            dsum = ektr[:, 0:8].rearrange("p (a b) -> p a b", a=4).unsqueeze(3)
            P.do("ve", lambda v, s=srcv, d=dsum: v.tensor_reduce(
                d, s, AXX, ALU.add))
            P.do("ve", lambda v, d=dsum: v.reciprocal(
                den8[:, :].rearrange("p (a b) -> p a b", a=4).unsqueeze(3), d))
            fkdst = (fkF[:, :].rearrange("p (j c) -> p j c", j=JT)
                     [:, g * 4:(g + 1) * 4, pair * 128:(pair + 1) * 128]
                     .rearrange("p j (b f) -> p j b f", b=2))
            fk_rel = P.do("ve", lambda v, s=srcv, o=fkdst: v.tensor_tensor(
                o, s,
                den8[:, :].rearrange("p (a b) -> p a b", a=4)
                .unsqueeze(3).broadcast_to([128, 4, 2, 64]),
                ALU.mult), inc=eV)
        P.wait_tag("te", fk_rel)
        for jt in range(JT):
            sl = PT[jt % 2]
            P.wait_tag("te", trel[jt % 2])
            tt = P.do("te", lambda t, j=jt, s=sl, p=pair: t.transpose(
                s[:, 0:128],
                fkF[:, j * D + p * 128:j * D + p * 128 + 128],
                sident[:, :]), inc=eT)
            P.wait_tag("ve", tt)
            if len(fkT_wr) >= 2:
                P.wait_tag("ve", fkT_wr[-2])
            evt = P.do("ve", lambda v, s=sl, bb=jt % 2: v.tensor_copy(
                fkTst[bb][:, :], s[:, 0:128]), inc=eV)
            trel[jt % 2] = evt
            P.wait_tag("gp", evt)
            wrt = P.dma("gp", qG, fkT_s[jt, pair], fkTst[jt % 2][:, :])
            fkT_wr.append(wrt)
        gamma_step()
        gamma_step()

    fk_all_v = (eV, eV.n)
    P.wait_tag("gp", fk_all_v)
    for jt in range(JT):
        P.dma("gp", qG, fk_s[jt], fkF[:, jt * D:(jt + 1) * D])

    P.mark("A2")
    eq_wr = []
    zrel = None
    for pair in range(PAIRS):
        b = pair % 2
        if pair == 0:
            P.wait_tag("sy", wq_ready)
        ld = load_wcol(pair, wq_bf, wcol[b], wrel[b])
        ev = proj_pair(wcol[b], kqT, ld, ev_prev)
        ev_prev = ev
        wrel[b] = ev
        P.wait_tag("te", ev)
        for chn in range(4):
            P.wait_tag("te", zrel)
            mm = P.do("te", lambda t, n=chn, p=pair: t.matmul(
                PB[4][:, :],
                sfq[:, p * 128:(p + 1) * 128],
                kqT[:, n * 512:(n + 1) * 512],
                start=True, stop=True, skip_group_check=True), inc=eT)
            P.wait_tag("sc", mm)
            if len(eq_wr) >= 2:
                P.wait_tag("sc", eq_wr[-2][0])
            bb = chn % 2
            P.do("sc", lambda s, b2=bb: s.activation(
                eqst[b2][0:64, :].rearrange("p (j c) -> p j c", c=256)[:, :, 0:128],
                PB[4][0:64, :].rearrange("p (j c) -> p j c", c=128),
                AF.Exp))
            zrel = P.do("sc", lambda s, b2=bb: s.activation(
                eqst[b2][64:128, :].rearrange("p (j c) -> p j c", c=256)[:, :, 128:256],
                PB[4][64:128, :].rearrange("p (j c) -> p j c", c=128),
                AF.Exp), inc=eA)
            P.wait_tag("gp", zrel)
            wrt = None
            for j4 in range(4):
                jt = chn * 4 + j4
                wrt = P.dma("gp", qG, eq_s[jt, pair],
                            eqst[chn % 2][:, j4 * 256:(j4 + 1) * 256])
            eq_wr.append((wrt, zrel))
        gamma_step()
        gamma_step()

    P.mark("A3")
    P.barrier()
    esA2.close()
    esA4 = ExitStack()
    wvb = esA4.enter_context(nc.sbuf_tensor("wvb", [128, DP * 512], BF16))
    vbuf = [esA4.enter_context(nc.sbuf_tensor(f"vbuf{i}", [128, 520], BF16))
            for i in range(2)]
    tcsb = esA4.enter_context(nc.sbuf_tensor("tcsb", [128, PAIRS * 65], F32))

    def sbank(pair):
        if pair < 6:
            return PB[0], pair * 65
        if pair < 12:
            return PB[1], (pair - 6) * 65
        return PB[5], (pair - 12) * 65

    for i in range(2):
        P.do("ve", lambda v, b=vbuf[i]: v.memset(
            b[:, :].rearrange("p (a c) -> p a c", c=65)[:, :, 64:65], 1.0))
    P.do("ve", lambda v: v.memset(tcsb[:, :], 0.0), inc=eV)

    def sruns(isl):
        runs = []
        for pp in range(4):
            pair = isl * 4 + pp
            bk, col = sbank(pair)
            if runs and runs[-1][0] is bk and runs[-1][1] + runs[-1][3] == col:
                runs[-1] = (bk, runs[-1][1], runs[-1][2], runs[-1][3] + 65)
            else:
                runs.append((bk, col, pair * 65, 65))
        return runs

    wv_rel = None
    vwr = [None, None]
    sacc_rel = None
    ev_hist = [None, None]
    for isl in range(4):
        P.wait_tag("sy", wv_rel)
        if isl == 0:
            P.wait_tag("sy", wv_ready)
        for g in range(4):
            P.dma("sy", qS,
                  wvb[:, g * 4 * 512:(g + 1) * 4 * 512]
                  .rearrange("p (a i) -> p a i", a=4),
                  wv_bf[g * 512:(g + 1) * 512, isl * 512:(isl + 1) * 512]
                  .rearrange("(a p) i -> p a i", p=128))
        ldt = (qS, qS.n)
        P.wait_tag("te", ldt)
        for jt in range(JT):
            bank = PB[2 + (jt % 2)]
            b = jt % 2
            P.wait_tag("te", ev_hist[b])
            mm = None
            for kp in range(DP):
                mm = P.do("te", lambda t, k=kp, j=jt, bk=bank: t.matmul(
                    bk[:, :], hT[:, k * T + j * 128:k * T + (j + 1) * 128],
                    wvb[:, k * 512:(k + 1) * 512],
                    start=(k == 0), stop=(k == DP - 1),
                    skip_group_check=True), inc=eT)
            P.wait_tag("ve", mm)
            P.wait_tag("ve", vwr[b])
            evt = P.do("ve", lambda v, bk=bank, bb=b: v.tensor_copy(
                vbuf[bb][:, :].rearrange("p (a c) -> p a c", c=65)[:, :, 0:64],
                bk[:, :].rearrange("p (a c) -> p a c", c=64)), inc=eV)
            ev_hist[b] = evt
            P.wait_tag("te", evt)
            P.wait_tag("te", sacc_rel)
            smm = None
            for pp in range(4):
                pair = isl * 4 + pp
                bk, col = sbank(pair)
                for hh in range(2):
                    smm = P.do("te", lambda t, j=jt, p_=pair, pp_=pp, h=hh,
                               b_=bk, c=col, bb=b: t.matmul(
                                   b_[64 * h:64 * (h + 1), c:c + 65],
                                   fkF[:, j * D + (p_ * 2 + h) * 64:
                                       j * D + (p_ * 2 + h) * 64 + 64],
                                   vbuf[bb][:, (pp_ * 2 + h) * 65:
                                            (pp_ * 2 + h + 1) * 65],
                                   start=True, stop=True,
                                   skip_group_check=True,
                                   tile_position=(0, 64 * h)),
                               inc=(eT if (pp == 3 and hh == 1) else None))
            P.wait_tag("ve", smm)
            for bk, bcol, gcol_, wdt in sruns(isl):
                sacc_rel = P.do("ve", lambda v, b_=bk, c=bcol, g2=gcol_,
                                w=wdt: v.tensor_add(
                                    tcsb[:, g2:g2 + w], tcsb[:, g2:g2 + w],
                                    b_[:, c:c + w]), inc=eV)
            P.wait_tag("gp", smm)
            vwr[b] = P.dma("gp", qG, v_s[jt][:, isl * 520:(isl + 1) * 520],
                           vbuf[b][:, :])
        wv_rel = (eT, eT.n)
        gamma_step()

    P.mark("A4")
    st_done = (eV, eV.n)
    P.wait_tag("ve", st_done)
    tcv = P.do("ve", lambda v: v.memset(s_init[:, :], 0.0), inc=eV)
    P.wait_tag("sy", tcv)
    P.dma("sy", qS, cc_in_t.ap(), tcsb[:, :])
    P.wait_tag("gp", (qS, qS.n))
    P.do("gp", lambda g: g.collective_compute(
        "AllGather", ALU.bypass,
        replica_groups=[[0, 1, 2, 3], [4, 5, 6, 7]],
        ins=[cc_in_t.ap().opt()],
        outs=[cc_out_t.ap().opt()]), inc=ccs)
    P.wait_tag("sy", (ccs, ccs.n))
    for r in range(4):
        P.dma("sy", qS, tcsb[:, :], cc_out_t.ap()[r * 128:(r + 1) * 128, :])
        P.wait_tag("ve", (qS, qS.n))
        acc = P.do("ve", lambda v, rr=r: v.scalar_tensor_tensor(
            s_init[:, :], tcsb[:, :], scmask[:, rr:rr + 1], s_init[:, :],
            ALU.mult, ALU.add), inc=eV)
        P.wait_tag("sy", acc)
    P.do("ve", lambda v: v.tensor_scalar_add(
        s_init[:, :].rearrange("p (a c) -> p a c", c=65)[:, :, 64:65],
        s_init[:, :].rearrange("p (a c) -> p a c", c=65)[:, :, 64:65],
        EPS_ATT), inc=eV)

    while gamma_pending():
        gamma_step()
    P.wait_tag("ve", (eA, eA.n))
    for m in range(3):
        P.do("ve", lambda v, mm=m: v.reduce_sum(
            gcol[:, mm:mm + 1], gpart[:, mm * 176:(mm + 1) * 176], AXX))
    gv = P.do("ve", lambda v: v.memset(gcol[:, 3:4], 0.0), inc=eV)
    P.wait_tag("te", gv)
    P.wait_tag("te", (eA, eA.n))
    g1 = P.do("te", lambda t: t.matmul(PB[4][0:1, 0:4], onecol_f[:, :],
                                       gcol[:, 0:4], start=True, stop=True,
                                       skip_group_check=True), inc=eT)
    P.wait_tag("ve", g1)
    P.do("ve", lambda v: v.tensor_scalar(grow[0:1, 0:4], PB[4][0:1, 0:4],
                                         1.0 / NW_EL, EPS_Q, ALU.mult, ALU.add))
    gv2 = P.do("ve", lambda v: v.reciprocal(grow[0:1, 0:4], grow[0:1, 0:4]),
               inc=eV)
    P.wait_tag("te", gv2)
    g2 = P.do("te", lambda t: t.matmul(PB[4][:, 0:4], ones_f[0:1, :],
                                       grow[0:1, 0:4], start=True, stop=True,
                                       skip_group_check=True), inc=eT)
    P.wait_tag("sc", g2)
    P.do("sc", lambda s: s.copy(r_bc[:, :], PB[4][:, 0:4]), inc=eA)
    esG.close()

    P.mark("A5")
    P.barrier()
    esA4.close()
    woT_sb = hT
    h2T = fkF

    esB1 = ExitStack()
    eqj = esB1.enter_context(nc.sbuf_tensor("eqj", [128, PAIRS * 256], BF16))
    fkTj = esB1.enter_context(nc.sbuf_tensor("fkTj", [128, PAIRS * 128], BF16))
    fkj = esB1.enter_context(nc.sbuf_tensor("fkj", [128, D], BF16))
    vj = esB1.enter_context(nc.sbuf_tensor("vj", [128, H * 65], BF16))
    atm = esB1.enter_context(nc.sbuf_tensor("atm", [128, 256], BF16))
    ypair = esB1.enter_context(nc.sbuf_tensor("ypair", [128, 128], BF16))
    recip2 = esB1.enter_context(nc.sbuf_tensor("recip2", [128, 2], F32))
    yTc = esB1.enter_context(nc.sbuf_tensor("yTc", [128, PAIRS * 512], BF16))
    xtile = esB1.enter_context(nc.sbuf_tensor("xtile", [128, 512], F32))
    r1b = [esB1.enter_context(nc.sbuf_tensor("r1bx", [128, 512], F32))] * 1
    sxb2 = esB1.enter_context(nc.sbuf_tensor("sxb2x", [128, 512], BF16))
    sx22 = esB1.enter_context(nc.sbuf_tensor("sx22x", [128, 512], BF16))
    rtile = esB1.enter_context(nc.sbuf_tensor("rtilex", [128, 512], F32))

    P.wait_tag("sy", wo_ready)
    for kp in range(DP):
        P.dma("sy", qS, woT_sb[:, kp * 2048:(kp + 1) * 2048],
              wo_bf[kp * 128:(kp + 1) * 128, :])
    woT_ld = (qS, qS.n)

    in_rel = {"all": None}
    in_tags = {}

    def load_jtile(jt):
        P.wait_tag("sy", in_rel["all"])
        for g in range(4):
            P.dma("sy", qS,
                  eqj[:, g * 1024:(g + 1) * 1024]
                  .rearrange("p (g t) -> p g t", g=4),
                  eq_s[jt, 4 * g:4 * (g + 1)].rearrange("g p t -> p g t"))
        for g in range(4):
            P.dma("sy", qS,
                  fkTj[:, g * 512:(g + 1) * 512]
                  .rearrange("p (g t) -> p g t", g=4),
                  fkT_s[jt, 4 * g:4 * (g + 1)].rearrange("g p t -> p g t"))
        t1 = (qS, qS.n)
        P.wait_tag("sc", in_rel["all"])
        P.dma("sc", qA, fkj[:, :], fk_s[jt])
        P.dma("sc", qA, vj[:, :], v_s[jt])
        in_tags[jt] = (t1, (qA, qA.n))

    load_jtile(0)
    prev_mm4 = None
    at_rel = [None, None]
    prev_ymm = None
    prev_tp = None
    tp_rel = [None, None]
    ln2_rel = [None, None]
    for ch in range(NCH):
        for j4 in range(4):
            jt = ch * 4 + j4
            b = jt % 2
            P.wait_tag("ve", in_tags[jt][0])
            P.wait_tag("ve", in_tags[jt][1])
            P.wait_tag("te", in_tags[jt][0])
            P.wait_tag("te", in_tags[jt][1])
            P.wait_tag("ve", prev_mm4)
            snap_tag = P.do("ve", lambda v: v.tensor_copy(
                s_snap[:, :], s_init[:, :]), inc=eV)
            P.wait_tag("te", snap_tag)
            if j4 == 0:
                P.wait_tag("te", (eA, eA.n))

            mm4tag = None
            for pair in range(PAIRS):
                ab = PB[2 + (pair % 2)]
                P.wait_tag("te", at_rel[pair % 2])
                at_mm = P.do("te", lambda t, p=pair, bk=ab: t.matmul(
                    bk[:, 0:256],
                    fkTj[:, p * 128:(p + 1) * 128],
                    eqj[:, p * 256:(p + 1) * 256],
                    start=True, stop=True, skip_group_check=True), inc=eT)
                P.wait_tag("ve", at_mm)
                P.wait_tag("ve", prev_ymm)
                atv = P.do("ve", lambda v, bk=ab: v.tensor_tensor(
                    atm[:, :], bk[:, 0:256], smask[:, :], ALU.mult), inc=eV)
                at_rel[pair % 2] = atv
                P.wait_tag("te", atv)
                ymm = None
                for hh in range(2):
                    P.do("te", lambda t, p=pair, h=hh: t.matmul(
                        PB[4][:, h * 65:(h + 1) * 65],
                        atm[:, h * 128:(h + 1) * 128],
                        vj[:, (2 * p + h) * 65:(2 * p + h + 1) * 65],
                        start=True, stop=False, skip_group_check=True))
                    ymm = P.do("te", lambda t, p=pair, h=hh: t.matmul(
                        PB[4][:, h * 65:(h + 1) * 65],
                        eqj[:, p * 256 + h * 128:p * 256 + (h + 1) * 128],
                        s_snap[:, p * 65:(p + 1) * 65],
                        start=False, stop=True, skip_group_check=True),
                        inc=(eT if hh == 1 else None))
                prev_ymm = ymm
                for hh in range(2):
                    bk, col = sbank(pair)
                    mm4tag = P.do("te", lambda t, p=pair, h=hh, b_=bk, c=col:
                                  t.matmul(
                                      b_[64 * h:64 * (h + 1), c:c + 65],
                                      fkj[:, (2 * p + h) * 64:
                                          (2 * p + h + 1) * 64],
                                      vj[:, (2 * p + h) * 65:
                                         (2 * p + h + 1) * 65],
                                      start=True, stop=True,
                                      skip_group_check=True,
                                      tile_position=(0, 64 * h)),
                                  inc=(eT if hh == 1 else None))
                P.wait_tag("ve", ymm)
                P.wait_tag("ve", prev_tp)
                P.do("ve", lambda v: v.reciprocal(recip2[:, :],
                                                  PB[4][:, 64:130:65]))
                P.do("ve", lambda v: v.tensor_scalar_mul(
                    ypair[:, 0:64], PB[4][:, 0:64], recip2[:, 0:1]))
                ynorm = P.do("ve", lambda v: v.tensor_scalar_mul(
                    ypair[:, 64:128], PB[4][:, 65:129], recip2[:, 1:2]),
                    inc=eV)
                P.wait_tag("te", ynorm)
                P.wait_tag("te", tp_rel[pair % 2])
                tp = P.do("te", lambda t, p=pair: t.transpose(
                    PT[p % 2][:, 0:128], ypair[:, :], sident[:, :]),
                    inc=eT)
                prev_tp = tp
                P.wait_tag("ve", tp)
                tp_rel[pair % 2] = P.do("ve", lambda v, p=pair, jj=j4:
                                        v.tensor_copy(
                                            yTc[:, p * 512 + jj * 128:
                                                p * 512 + (jj + 1) * 128],
                                            PT[p % 2][:, 0:128]), inc=eV)
            P.wait_tag("ve", mm4tag)
            for bk, bcol, gc, wdt in ((PB[0], 0, 0, 390), (PB[1], 0, 390, 390),
                                      (PB[5], 0, 780, 260)):
                prev_mm4 = P.do("ve", lambda v, b_=bk, c=bcol, g2=gc, w=wdt:
                                v.tensor_add(s_init[:, g2:g2 + w],
                                             s_init[:, g2:g2 + w],
                                             b_[:, c:c + w]), inc=eV)
            in_rel["all"] = (eT, eT.n)
            if jt + 1 < JT:
                load_jtile(jt + 1)

        yT_done = (eV, eV.n)
        P.wait_tag("te", yT_done)
        if ch == 0:
            P.wait_tag("te", woT_ld)
        wo_rel = [None, None]
        xrel = None
        r1rel = None
        for o in range(DP):
            b = o % 2
            P.wait_tag("sy", xrel)
            xt = P.dma("sy", qS, xtile[:, :],
                       xT[o * 128:(o + 1) * 128, ch * 512:(ch + 1) * 512])
            bank = PB[2 + b]
            P.wait_tag("te", wo_rel[b])
            mm = None
            for kp in range(DP):
                mm = P.do("te", lambda t, k=kp, oo=o, bk=bank: t.matmul(
                    bk[:, :], woT_sb[:, k * 2048 + oo * 128:
                                     k * 2048 + (oo + 1) * 128],
                    yTc[:, k * 512:(k + 1) * 512],
                    start=(k == 0), stop=(k == DP - 1),
                    skip_group_check=True), inc=eT)
            P.wait_tag("ve", mm)
            P.wait_tag("ve", xt)
            P.wait_tag("ve", r1rel)
            r1 = P.do("ve", lambda v, bk=bank: v.tensor_add(
                r1b[0][:, :], bk[:, :], xtile[:, :]), inc=eV)
            xrel = r1
            wo_rel[b] = r1
            P.wait_tag("gp", r1)
            r1rel = P.dma("gp", qG,
                          resid1_s[o * 128:(o + 1) * 128,
                                   ch * 512:(ch + 1) * 512],
                          r1b[0][:, :])
        chunk_r1w = (qG, qG.n)
        wo_done = (eV, eV.n)

        P.wait_tag("te", wo_done)
        strel_v = None
        strel_a = None
        sxrel = None
        sqrel = None
        stat_tag = None
        P.wait_tag("te", (eA, eA.n))
        P.do("te", lambda t: t.matmul(
            PB[2][0:1, 0:512], zcol_bf[0:1, 0:1], zrow_bf[0:1, 0:512],
            start=True, stop=False, skip_group_check=True))
        P.do("te", lambda t: t.matmul(
            PB[3][0:1, 0:512], zcol_bf[0:1, 0:1], zrow_bf[0:1, 0:512],
            start=True, stop=False, skip_group_check=True))
        for dp in range(DP):
            P.wait_tag("sy", chunk_r1w)
            P.wait_tag("sy", strel_v)
            P.wait_tag("sy", strel_a)
            rt = P.dma("sy", qS, rtile[:, :],
                       resid1_s[dp * 128:(dp + 1) * 128,
                                ch * 512:(ch + 1) * 512])
            P.wait_tag("ve", rt)
            P.wait_tag("ve", sxrel)
            sx = P.do("ve", lambda v: v.tensor_copy(
                sxb2[:, :], rtile[:, :]), inc=eV)
            strel_v = sx
            P.wait_tag("sc", rt)
            P.wait_tag("sc", sqrel)
            sq = P.do("sc", lambda s: s.activation(
                sx22[:, :], rtile[:, :], AF.Square), inc=eA)
            strel_a = sq
            P.wait_tag("te", sx)
            P.wait_tag("te", sq)
            sxrel = P.do("te", lambda t, d=dp: t.matmul(
                PB[2][0:1, 0:512], ones_bf[:, :], sxb2[:, :],
                start=False, stop=(d == DP - 1),
                skip_group_check=True), inc=eT)
            stat_tag = sqrel = P.do("te", lambda t, d=dp: t.matmul(
                PB[3][0:1, 0:512], ones_bf[:, :], sx22[:, :],
                start=False, stop=(d == DP - 1),
                skip_group_check=True), inc=eT)
        ev2 = ln_rows_and_bcast(512, PB[2][0:1, 0:512], PB[3][0:1, 0:512],
                                PB[2][:, 0:512], PB[3][:, 0:512], stat_tag)
        P.wait_tag("ve", ev2)
        arel = None
        P.wait_tag("sy", strel_v)
        P.wait_tag("sy", strel_a)
        for dp in range(DP):
            P.wait_tag("sy", arel)
            rt = P.dma("sy", qS, rtile[:, :],
                       resid1_s[dp * 128:(dp + 1) * 128,
                                ch * 512:(ch + 1) * 512])
            P.wait_tag("ve", rt)
            P.do("ve", lambda v: v.tensor_mul(
                sxb2[:, :], rtile[:, :], ab_sb[:, 0:512]))
            arel = P.do("ve", lambda v, d=dp, c=ch: v.tensor_add(
                h2T[:, d * T + c * 512:d * T + (c + 1) * 512],
                sxb2[:, :], ab_sb[:, 512:1024]), inc=eV)

    P.mark("B1")
    P.barrier()
    esB1.close()
    esB2 = ExitStack()
    w1c = esB2.enter_context(nc.sbuf_tensor("w1c", [128, 2048], F32))
    w2c = esB2.enter_context(nc.sbuf_tensor("w2c", [128, 2048], F32))
    w3r = esB2.enter_context(nc.sbuf_tensor("w3r", [128, 1024], F32))
    qtmp = esB2.enter_context(nc.sbuf_tensor("qtmp", [128, 1024], F32))
    qtmpb = esB2.enter_context(nc.sbuf_tensor("qtmpb", [128, 1024], BF16))
    qw1c = esB2.enter_context(nc.sbuf_tensor("qw1c", [128, 2048], BF16))
    qw2c = esB2.enter_context(nc.sbuf_tensor("qw2c", [128, 2048], BF16))
    qw3r = esB2.enter_context(nc.sbuf_tensor("qw3r", [128, 2048], BF16))
    sgm = esB2.enter_context(nc.sbuf_tensor("sgm", [128, 512], F32))
    hidb = esB2.enter_context(nc.sbuf_tensor("hidbx", [128, 512], BF16))

    def quant_half(src_ap, dst_ap, ridx, deps_sc, deps_ve, prev_i3):
        for t in deps_sc:
            P.wait_tag("sc", t)
        P.wait_tag("sc", prev_i3)
        i1 = P.do("sc", lambda s: s.activation(
            qtmp[:, :], src_ap, AF.Identity,
            scale=r_bc[:, ridx:ridx + 1], bias=cst[:, 1:2]), inc=eA)
        for t in deps_ve:
            P.wait_tag("ve", t)
        P.wait_tag("ve", i1)
        P.do("ve", lambda v: v.tensor_scalar(
            qtmpb[:, :], qtmp[:, :], -MAGIC, -1.0, ALU.add, ALU.max))
        i3 = P.do("ve", lambda v: v.tensor_scalar_min(
            dst_ap, qtmpb[:, :], 1.0), inc=eV)
        return i1, i3

    def quant(srcb, dst, ridx, deps_sc, deps_ve, prev_i3):
        i1 = i3 = prev_i3
        for hf in range(2):
            sl = slice(hf * 1024, (hf + 1) * 1024)
            i1, i3 = quant_half(srcb[:, sl], dst[:, sl], ridx,
                                deps_sc, deps_ve, i3)
        return i1, i3

    ld_w1 = ld_w2 = ld_w3 = None
    q_prev = None
    xmm_prev = None
    qw3dma_prev = None
    sg_prev_hv = None
    hid_rel = [None, None]
    x_rel = {}
    for fh in range(FHP):
        P.wait_tag("sy", ld_w1)
        for g in range(4):
            P.dma("sy", qS,
                  w1c[:, g * 512:(g + 1) * 512]
                  .rearrange("p (a i) -> p a i", a=4),
                  w1T[g * 512:(g + 1) * 512, fh * 128:(fh + 1) * 128]
                  .rearrange("(a p) i -> p a i", p=128))
        t1 = (qS, qS.n)
        P.wait_tag("sy", ld_w2)
        for g in range(4):
            P.dma("sy", qS,
                  w2c[:, g * 512:(g + 1) * 512]
                  .rearrange("p (a i) -> p a i", a=4),
                  w2T[g * 512:(g + 1) * 512, fh * 128:(fh + 1) * 128]
                  .rearrange("(a p) i -> p a i", p=128))
        t2 = (qS, qS.n)
        i1a, q1 = quant(w1c[:, :], qw1c[:, :], 0, [t1], [xmm_prev], q_prev)
        i1b, q2 = quant(w2c[:, :], qw2c[:, :], 1, [t2], [xmm_prev], q1)
        q3 = q2
        i1c = None
        for hf in range(2):
            P.wait_tag("sc", ld_w3)
            t3 = P.dma("sc", qA, w3r[:, :],
                       w3T[fh * 128:(fh + 1) * 128,
                           hf * 1024:(hf + 1) * 1024])
            i1c, q3 = quant_half(w3r[:, :],
                                 qw3r[:, hf * 1024:(hf + 1) * 1024], 2,
                                 [t3], [qw3dma_prev], q3)
            ld_w3 = i1c
        ld_w1, ld_w2 = i1a, i1b
        q_prev = q3
        P.wait_tag("gp", q3)
        for ob in range(4):
            qw3dma_prev = P.dma("gp", qG,
                                qw3_s[ob][fh * 128:(fh + 1) * 128, :],
                                qw3r[:, ob * 512:(ob + 1) * 512])
        P.wait_tag("te", q1)
        P.wait_tag("te", q2)
        mm2 = None
        for n in range(NCH):
            b1 = PB[2] if n % 2 == 0 else PB[0]
            b2 = PB[3] if n % 2 == 0 else PB[1]
            P.wait_tag("te", x_rel.get(n % 2))
            mm1 = None
            for kp in range(DP):
                mm1 = P.do("te", lambda t, k=kp, nn=n, bk=b1: t.matmul(
                    bk[:, :], qw1c[:, k * 128:(k + 1) * 128],
                    h2T[:, k * T + nn * 512:k * T + (nn + 1) * 512],
                    start=(k == 0), stop=(k == DP - 1),
                    skip_group_check=True), inc=(eT if kp == DP - 1 else None))
            for kp in range(DP):
                mm2 = P.do("te", lambda t, k=kp, nn=n, bk=b2: t.matmul(
                    bk[:, :], qw2c[:, k * 128:(k + 1) * 128],
                    h2T[:, k * T + nn * 512:k * T + (nn + 1) * 512],
                    start=(k == 0), stop=(k == DP - 1),
                    skip_group_check=True), inc=(eT if kp == DP - 1 else None))
            P.wait_tag("sc", mm1)
            P.wait_tag("sc", sg_prev_hv)
            sg = P.do("sc", lambda s, bk=b1: s.activation(
                sgm[:, :], bk[:, :], AF.Sigmoid), inc=eA)
            P.wait_tag("ve", sg)
            P.wait_tag("ve", mm2)
            P.wait_tag("ve", hid_rel[0])
            hv = P.do("ve", lambda v, bk=b2: v.tensor_mul(
                hidb[:, :], bk[:, :], sgm[:, :]), inc=eV)
            x_rel[n % 2] = hv
            sg_prev_hv = hv
            P.wait_tag("gp", hv)
            hid_rel[0] = P.dma("gp", qG,
                               hid_s[n][fh * 128:(fh + 1) * 128, :],
                               hidb[:, :])
        xmm_prev = mm2

    P.mark("B2")
    P.barrier()
    esB2.close()
    esBig.close()
    esB3 = ExitStack()
    hidc = esB3.enter_context(nc.sbuf_tensor("hidc", [128, FHP * 512], BF16))
    qw3b = [esB3.enter_context(nc.sbuf_tensor(f"qw3b{i}", [128, FHP * 512],
                                              BF16)) for i in range(2)]
    rt3 = [esB3.enter_context(nc.sbuf_tensor(f"rt3{i}", [128, 512], F32))
           for i in range(2)]
    outb = [esB3.enter_context(nc.sbuf_tensor(f"outb{i}", [128, 512], F32))
            for i in range(2)]

    hid_ld_rel = None
    qw3_ld_rel = [None, None]
    rrel = [None, None]
    orel = [None, None]
    for ch in range(NCH):
        P.wait_tag("sc", hid_ld_rel)
        for fp in range(FHP):
            P.dma("sc", qA, hidc[:, fp * 512:(fp + 1) * 512],
                  hid_s[ch][fp * 128:(fp + 1) * 128, :])
        hid_ld = (qA, qA.n)
        P.wait_tag("te", hid_ld)
        for ob in range(4):
            qb = ob % 2
            P.wait_tag("sy", qw3_ld_rel[qb])
            for fp in range(FHP):
                P.dma("sy", qS, qw3b[qb][:, fp * 512:(fp + 1) * 512],
                      qw3_s[ob][fp * 128:(fp + 1) * 128, :])
            qw3_ld = (qS, qS.n)
            P.wait_tag("te", qw3_ld)
            mmlast = None
            for osub in range(4):
                o = ob * 4 + osub
                b = osub % 2
                bank = PB[2 + b]
                P.wait_tag("sy", rrel[b])
                rt = P.dma("sy", qS, rt3[b][:, :],
                           resid1_s[o * 128:(o + 1) * 128,
                                    ch * 512:(ch + 1) * 512])
                P.wait_tag("te", orel[b])
                mm = None
                for fp in range(FHP):
                    mm = P.do("te", lambda t, f=fp, oo=osub, bk=bank, q=qb:
                              t.matmul(
                                  bk[:, :],
                                  qw3b[q][:, f * 512 + oo * 128:
                                          f * 512 + (oo + 1) * 128],
                                  hidc[:, f * 512:(f + 1) * 512],
                                  start=(f == 0), stop=(f == FHP - 1),
                                  skip_group_check=True),
                              inc=(eT if fp == FHP - 1 else None))
                mmlast = mm
                P.wait_tag("ve", mm)
                P.wait_tag("ve", rt)
                P.wait_tag("ve", orel[b])
                ov = P.do("ve", lambda v, bk=bank, bb=b: v.tensor_add(
                    outb[bb][:, :], bk[:, :], rt3[bb][:, :]), inc=eV)
                rrel[b] = ov
                orel[b] = ov
                P.wait_tag("gp", ov)
                ow = P.dma("gp", qOUT,
                           outT[o * 128:(o + 1) * 128,
                                ch * 512:(ch + 1) * 512],
                           outb[b][:, :])
                orel[b] = ow
            qw3_ld_rel[qb] = mmlast
        hid_ld_rel = mmlast

    P.mark("B3")
    P.barrier()
    esB3.close()
    if stop and stop.startswith("seq:"):
        P.emit(stop_seq=int(stop[4:]))
    else:
        P.emit(stop_seq=(P.marks.get(stop) if stop else None))
    import sys
    print("MARKS:", P.marks, file=sys.stderr)
    es.close()
    return nc


_NC_CACHE = {}


def kernel(**inputs):
    x = np.asarray(inputs["L0_attn_hidden_states"])
    Wq = np.asarray(inputs["L0_attn_Wq"])
    Wk = np.asarray(inputs["L0_attn_Wk"])
    Wv = np.asarray(inputs["L0_attn_Wv"])
    Wo = np.asarray(inputs["L0_attn_Wo"])
    Wfq = np.asarray(inputs["L0_attn_Wfq"])
    Wfk = np.asarray(inputs["L0_attn_Wfk"])
    w1 = np.asarray(inputs["L0_ffn_w1"])
    w2 = np.asarray(inputs["L0_ffn_w2"])
    w3 = np.asarray(inputs["L0_ffn_w3"])

    C = np.ascontiguousarray
    wqT = C(Wq.T)
    wkT = C(Wk.T)
    wvT = C(Wv.T)
    woT = C(Wo.T)
    w1T = C(w1.T)
    w2T = C(w2.T)
    w3T = C(w3.T)
    def packwf(Wf):
        out = np.zeros((128, PAIRS * 128), np.float32)
        for p in range(PAIRS):
            out[0:64, p * 128:p * 128 + 64] = Wf[2 * p]
            out[64:128, p * 128 + 64:(p + 1) * 128] = Wf[2 * p + 1]
        return out
    wfq_p = packwf(Wfq)
    wfk_p = packwf(Wfk)
    tri = np.triu(np.ones((128, 128), np.float32))
    mask2 = np.concatenate([tri, tri], axis=1)
    ident = np.eye(128, dtype=np.float32)

    import os
    stop = os.environ.get("K_STOP") or None
    if "nc" not in _NC_CACHE or _NC_CACHE.get("stop") != stop:
        _NC_CACHE["nc"] = build(stop)
        _NC_CACHE["stop"] = stop
    nc = _NC_CACHE["nc"]

    in_maps = []
    for c in range(8):
        bidx, chunk = c // 4, c % 4
        xTc = C(x[bidx, chunk * T:(chunk + 1) * T, :].T)
        cm = np.zeros((128, 4), np.float32)
        cm[:, :chunk] = 1.0
        in_maps.append({
            "xT": xTc, "wqT": wqT, "wkT": wkT, "wvT": wvT, "woT": woT,
            "wfq": wfq_p, "wfk": wfk_p, "w1T": w1T, "w2T": w2T, "w3T": w3T,
            "mask2": mask2, "ident": ident, "cmask": cm,
        })
    res = run_bass_kernel_spmd(nc, in_maps, core_ids=list(range(8)))
    out = np.empty((2, 8192, 2048), np.float32)
    for c in range(8):
        bidx, chunk = c // 4, c % 4
        out[bidx, chunk * T:(chunk + 1) * T, :] = res.results[c]["outT"].T
    return out
